# version 12 (final)
# baseline (speedup 1.0000x reference)
import contextlib
import numpy as np
import ml_dtypes

import concourse.bass as bass
import concourse.mybir as mybir
from concourse.bass_utils import run_bass_kernel_spmd

F32 = mybir.dt.float32
BF16 = mybir.dt.bfloat16
FP8 = mybir.dt.float8e4
AX = mybir.AxisListType.X
ALU = mybir.AluOpType
ACT = mybir.ActivationFunctionType

D = 4096
I = 11008
IC = 256
NCH = I // IC
NKW = 8
NF = D // (128 * NKW)
TPC = 1024
STOK = 512
NT = STOK // 128
NST = TPC // STOK
JB = 40
KTGT = I // 2
EPS_LN = 1e-5
EPS_RMS = 1.1920929e-07
TAU_INV = 10.0
NSK = 50
HC = 512
NHC = D // HC
NPAIR = I // 256


def build_nc(dbg=False):
    nc = bass.Bass("TRN2", target_bir_lowering=False,
                   detect_race_conditions=False)
    x_in = nc.dram_tensor("x_in", [TPC, D], F32, kind="ExternalInput").ap()
    wgT = nc.dram_tensor("wgT", [D, I], F32, kind="ExternalInput").ap()
    wuT = nc.dram_tensor("wuT", [D, I], F32, kind="ExternalInput").ap()
    wdT = nc.dram_tensor("wdT", [I, D], F32, kind="ExternalInput").ap()
    wb_h = nc.dram_tensor("wb", [1024, 1024], F32, kind="ExternalInput").ap()
    wallT = nc.dram_tensor("wallT", [D, 32], F32, kind="ExternalInput").ap()
    bres_h = nc.dram_tensor("bres", [128, 16], F32, kind="ExternalInput").ap()
    bpre_h = nc.dram_tensor("bpre", [128, 4], F32, kind="ExternalInput").ap()
    bpost_h = nc.dram_tensor("bpost", [128, 4], F32, kind="ExternalInput").ap()
    eye_h = nc.dram_tensor("eye4", [128, 16], F32, kind="ExternalInput").ap()
    gate_h = nc.dram_tensor("gate", [128, 1], F32, kind="ExternalInput").ap()
    ident_h = nc.dram_tensor("ident", [128, 128], BF16, kind="ExternalInput").ap()
    y_out = nc.dram_tensor("y_out", [TPC, D], F32, kind="ExternalOutput").ap()
    if dbg:
        dbg_s = nc.dram_tensor("dbg_s", [128, NT, I], BF16,
                               kind="ExternalOutput").ap()
        dbg_t = nc.dram_tensor("dbg_t", [128, 4], F32,
                               kind="ExternalOutput").ap()
        dbg_a = nc.dram_tensor("dbg_a", [128, NT, D], BF16,
                               kind="ExternalOutput").ap()
        dbg_x = nc.dram_tensor("dbg_x", [128, D], F32,
                               kind="ExternalOutput").ap()
        dbg_st = nc.dram_tensor("dbg_st", [128, 16], F32,
                               kind="ExternalOutput").ap()
        dbg_ln = nc.dram_tensor("dbg_ln", [128, D], BF16,
                               kind="ExternalOutput").ap()
        dbg_tr = nc.dram_tensor("dbg_tr", [128, 400], F32,
                               kind="ExternalOutput").ap()
        dbg_f = nc.dram_tensor("dbg_f", [128, NT, 96], F32,
                               kind="ExternalOutput").ap()
        dbg_g = nc.dram_tensor("dbg_g", [128, 8, 24], F32,
                               kind="ExternalOutput").ap()
    wgB = nc.dram_tensor("wgB", [D, I], BF16, kind="Internal").ap()
    wuB = nc.dram_tensor("wuB", [D, I], BF16, kind="Internal").ap()
    wdB = nc.dram_tensor("wdB", [I, D], BF16, kind="Internal").ap()

    es = contextlib.ExitStack()
    with es:
        sb = lambda n, s, d: es.enter_context(nc.sbuf_tensor(n, s, d))
        ps = lambda n, s, d: es.enter_context(nc.psum_tensor(n, s, d))
        sS = sb("sS", [128, NT, I], BF16)
        xa = sb("xa", [128, 32, STOK], BF16)
        b2 = sb("b2", [128, 2, I], BF16)
        junk = sb("junk", [128, I], FP8)
        wf32 = sb("wf32", [128, 2, IC * NKW], F32)
        wbf = sb("wbf", [128, 2, IC * NKW], BF16)
        wallB = sb("wallB", [128, 32, 32], BF16)
        ident = sb("ident_s", [128, 128], BF16)
        sc = sb("sc", [128, 4, IC], BF16)
        bres = sb("bres_s", [128, 16], F32)
        bpre = sb("bpre_s", [128, 4], F32)
        bpost = sb("bpost_s", [128, 4], F32)
        eye4 = sb("eye4_s", [128, 16], F32)
        gig = sb("gig", [128, 16], F32)
        gsc = sb("gsc", [128, 4], F32)
        st1 = sb("st1", [128, 16], F32)
        ceps = sb("ceps", [128, 1], F32)
        cepsr = sb("cepsr", [128, 1], F32)
        lohi = sb("lohi", [128, 8], F32)
        cnt2 = sb("cnt2", [128, 8], F32)
        cmsk = sb("cmsk", [128, 4], mybir.dt.int32)
        tthr = sb("tthr", [128, 4], F32)
        logi = sb("logi", [128, NT, 24], F32)
        zsk = sb("zsk", [128, NT, 16], F32)
        msk = sb("msk", [128, NT, 16], F32)
        lmu = sb("lmu", [128, NT, 4], F32)
        lmv = sb("lmv", [128, NT, 4], F32)
        ssk = sb("ssk", [128, NT, 4], F32)
        hrs = sb("hrs", [128, NT, 16], F32)
        hpre = sb("hpre", [128, NT, 4], F32)
        hpost = sb("hpost", [128, NT, 4], F32)
        m1s = sb("m1s", [128, NT, 4], F32)
        lmr = sb("lmr", [128, NT, 4], F32)
        xpre = sb("xpre", [128, 1024], F32)
        trc = sb("trc", [128, 400], F32) if dbg else None
        pA = [ps(f"pA{i_}", [128, 512], F32) for i_ in range(4)]
        pT = [ps(f"pT{i_}", [128, 128], F32) for i_ in range(2)]
        pJ = ps("pJ", [128, 64], F32)
        SEM = {}
        for s in ("qW", "qG", "vS", "sS", "tS"):
            SEM[s] = es.enter_context(nc.semaphore(s))
        blk = es.enter_context(nc.Block())

        sSf = sS[:].rearrange("p a b -> p (a b)")
        x1v = sSf[:, 0:NT * D].rearrange("p (a b) -> p a b", a=NT)
        xl2v = sSf[:, NT * D:2 * NT * D].rearrange("p (a b) -> p a b", a=NT)
        TAIL = 2 * NT * D
        lnb = sSf[:, TAIL:TAIL + D]
        xnb = sSf[:, TAIL + D:TAIL + 2 * D]
        xaf = xa[:].rearrange("p k t -> p (k t)")
        xnTs = [xaf[:, tau * D:(tau + 1) * D].rearrange("p (k t) -> p k t", k=32)
                for tau in range(NT)]
        xpb = sSf[:, TAIL + 2 * D:TAIL + 2 * D + 1024]
        xpT = sSf[:, TAIL + 2 * D + 1024:TAIL + 2 * D + 2048].rearrange(
            "p (k t) -> p k t", k=8)
        ybv = sSf[:, TAIL + 2 * D + 2048:TAIL + 2 * D + 3072]
        fov = lnb
        assert TAIL + 2 * D + 3072 <= NT * I
        attn = xa[:].rearrange("p k t -> p (k t)").rearrange(
            "p (a b) -> p a b", a=NT)
        b2f = b2[:].rearrange("p a b -> p (a b)")
        wbV = b2f[:, 0:8192].rearrange("p (k c) -> p k c", k=8)
        xtv = wf32[:].rearrange("p a b -> p (a b)")

        sched = {e: [] for e in ("sync", "gpsimd", "scalar", "vector", "tensor")}
        cnt = {s: 0 for s in SEM}
        all_waits = []

        _CHAIN = {"vector": "vS", "scalar": "sS"}

        def E(eng, fn, w=(), inc=None, noop=False):
            ww = [x for x in w if x is not None]
            ch = _CHAIN.get(eng)
            if ch is not None and not noop:
                if cnt[ch] > 0:
                    ww = [x for x in ww if x[0] != ch] + [(ch, cnt[ch])]
                if inc is None:
                    inc = (ch, 1)
                else:
                    assert inc[0] == ch
            all_waits.extend(ww)
            sched[eng].append((ww, fn, inc))
            if inc is not None:
                cnt[inc[0]] += inc[1]
            return (inc[0], cnt[inc[0]]) if inc is not None else None

        def snap():
            return dict(cnt)

        def phase_waits(s):
            return {k: (k, v) for k, v in s.items() if v > 0}

        def enter_phase(s, engines=("sync", "gpsimd", "scalar", "vector", "tensor")):
            pw = phase_waits(s)
            own = {"sync": "qW", "gpsimd": "qG", "scalar": "sS", "vector": "vS",
                   "tensor": "tS"}
            for eng in engines:
                ws = [pw[k] for k in pw if k != own[eng]]
                if ws:
                    E(eng, lambda e: None, w=ws, noop=True)

        def vpad():
            return E("vector", lambda e: e.tensor_copy(
                out=gsc[:, 3:4], in_=gsc[:, 1:2]), inc=("vS", 1))

        _ACT_SET = {ACT.Exp: "le", ACT.Ln: "le", ACT.Sqrt: "sq",
                    ACT.Silu: "si", ACT.Sigmoid: "sg", ACT.Sign: "sn"}
        _cur_set = [None]

        def act_guard(func):
            st_ = _ACT_SET.get(func)
            if st_ is None or st_ == _cur_set[0]:
                return
            _cur_set[0] = st_
            E("scalar", lambda e: e.activation(
                out=junk[:, 0:1], in_=ceps[:, 0:1], func=func))

        norm_by_bank = {0: None, 8: None}

        def tpad():
            return E("tensor", lambda e: e.matmul(
                out=pJ[:, 0:24], lhsT=ident[:], rhs=ident[:, 0:24],
                start=True, stop=True), inc=("tS", 1))

        def ln_prep(src_ap, waits, base, rms_only=False):
            c = lambda i: st1[:, base + i:base + i + 1]
            ww = list(waits)
            if norm_by_bank[base] is not None:
                ww.append(norm_by_bank[base])
            if not rms_only:
                E("scalar", lambda e: e.activation(
                    out=junk[:, 0:D], in_=src_ap, func=ACT.Copy,
                    accum_out=c(0)), w=ww, inc=("sS", 1))
                ww = []
            E("scalar", lambda e: e.activation(
                out=junk[:, 0:D], in_=src_ap, func=ACT.Square,
                accum_out=c(1)), w=ww, inc=("sS", 1))
            E("scalar", lambda e: e.mul(out=c(3), in_=c(1), mul=1.0 / D))
            if rms_only:
                return E("scalar", lambda e: e.activation(
                    out=c(6), in_=c(3), func=ACT.Sqrt,
                    bias=cepsr[:, 0:1]), inc=("sS", 1))
            E("scalar", lambda e: e.mul(out=c(2), in_=c(0), mul=1.0 / D))
            E("scalar", lambda e: e.mul(out=c(4), in_=c(2), mul=c(2)))
            E("scalar", lambda e: e.mul(out=c(4), in_=c(4), mul=-1.0))
            E("scalar", lambda e: e.add(out=c(5), in_=c(3), add=c(4)))
            act_guard(ACT.Sqrt)
            return E("scalar", lambda e: e.activation(
                out=c(6), in_=c(5), func=ACT.Sqrt,
                bias=ceps[:, 0:1]), inc=("sS", 1))

        def ln_norm(out_ap, src_ap, ev, base, rms_only=False, waits=()):
            c = lambda i: st1[:, base + i:base + i + 1]
            E("vector", lambda e: e.reciprocal(out=c(7), in_=c(6)),
              w=[ev] + list(waits))
            if rms_only:
                E("vector", lambda e: e.tensor_scalar(
                    out=out_ap, in0=src_ap, scalar1=c(7), scalar2=None,
                    op0=ALU.mult), inc=("vS", 1))
            else:
                E("vector", lambda e: e.tensor_scalar(
                    out=out_ap, in0=src_ap, scalar1=c(2), scalar2=c(7),
                    op0=ALU.subtract, op1=ALU.mult), inc=("vS", 1))
            p = vpad()
            norm_by_bank[base] = p
            return p

        e_ident = E("gpsimd", lambda e: e.dma_start(out=ident[:], in_=ident_h[:]),
                    inc=("qG", 16))
        e_bres = E("gpsimd", lambda e: e.dma_start(out=bres[:], in_=bres_h[:]),
                   inc=("qG", 16))
        e_bpre = E("gpsimd", lambda e: e.dma_start(out=bpre[:], in_=bpre_h[:]),
                   inc=("qG", 16))
        e_bpost = E("gpsimd", lambda e: e.dma_start(out=bpost[:], in_=bpost_h[:]),
                    inc=("qG", 16))
        e_eye = E("gpsimd", lambda e: e.dma_start(out=eye4[:], in_=eye_h[:]),
                  inc=("qG", 16))
        e_gate = E("gpsimd", lambda e: e.dma_start(out=gsc[:, 0:1], in_=gate_h[:]),
                   inc=("qG", 16))
        wallV = wallT.rearrange("(k p) n -> p k n", p=128)
        e_wall = E("gpsimd", lambda e: e.dma_start(
            out=wf32[:].rearrange("p a b -> p (a b)")[:, 0:1024].rearrange(
                "p (k n) -> p k n", n=32),
            in_=wallV), inc=("qG", 16))
        E("vector", lambda e: e.tensor_copy(
            out=wallB[:], in_=wf32[:].rearrange("p a b -> p (a b)")[:, 0:1024]
            .rearrange("p (k n) -> p k n", n=32)),
          w=[e_wall])
        act_guard(ACT.Sigmoid)
        E("scalar", lambda e: e.activation(out=gsc[:, 1:2], in_=gsc[:, 0:1],
                                           func=ACT.Sigmoid), w=[e_gate],
          inc=("sS", 1))
        sg = cnt["sS"]
        E("vector", lambda e: e.tensor_scalar(out=gsc[:, 2:3], in0=gsc[:, 1:2],
                                              scalar1=-1.0, scalar2=1.0,
                                              op0=ALU.mult, op1=ALU.add),
          w=[("sS", sg), e_eye])
        E("vector", lambda e: e.tensor_scalar(out=gig[:], in0=eye4[:],
                                              scalar1=gsc[:, 2:3], scalar2=None,
                                              op0=ALU.mult), inc=("vS", 1))
        E("vector", lambda e: e.memset(ceps[:], EPS_LN))
        E("vector", lambda e: e.memset(cepsr[:], EPS_RMS))

        for st in range(1 if dbg else NST):
            base_tok = st * STOK

            s0 = snap()
            enter_phase(s0)
            prev_gate = None
            for tau in range(NT):
                r0 = base_tok + tau * 128
                ex = E("gpsimd", lambda e, r0=r0: e.dma_start(
                    out=xtv, in_=x_in[r0:r0 + 128, :]),
                    w=[prev_gate] if prev_gate else [], inc=("qG", 16))
                bse = (tau % 2) * 8
                esd = ln_prep(xtv, [ex], bse)
                eln = ln_norm(lnb, xtv, esd, bse)
                prev_gate = eln
                for k in range(32):
                    b = k % 2
                    wtp = [eln]
                    if k >= 2:
                        wtp.append(evk[b])
                    et = E("tensor", lambda e, k=k, b=b: e.matmul(
                        out=pT[b][:], lhsT=lnb[:, 128 * k:128 * k + 128],
                        rhs=ident[:], start=True, stop=True),
                        w=wtp, inc=("tS", 1))
                    ee = E("vector", lambda e, k=k, b=b, tau=tau: e.tensor_copy(
                        out=xa[:, k, tau * 128:tau * 128 + 128], in_=pT[b][:]),
                        w=[et], inc=("vS", 1))
                    if k == 0:
                        evk = [None, None]
                    evk[b] = ee

            if dbg == 3:
                dsn = snap()
                enter_phase(dsn, engines=("gpsimd",))
                E("gpsimd", lambda e: e.dma_start(out=dbg_s, in_=sS[:]),
                  inc=("qG", 16))
                E("gpsimd", lambda e: e.dma_start(out=dbg_t, in_=tthr[:]),
                  inc=("qG", 16))
                E("gpsimd", lambda e: e.dma_start(out=dbg_a, in_=attn[:]),
                  inc=("qG", 16))
                E("gpsimd", lambda e: e.dma_start(out=dbg_x, in_=xtv),
                  inc=("qG", 16))
                E("gpsimd", lambda e: e.dma_start(out=dbg_st, in_=st1[:]),
                  inc=("qG", 16))
                E("gpsimd", lambda e: e.dma_start(out=dbg_ln, in_=lnb),
                  inc=("qG", 16))
                break
            def mm_phase(w_f32, w_bf16, is_g):
                ph = snap()
                enter_phase(ph)
                fill_ev = {}
                cons_ev = {}
                scr_ev = {}
                evac_ev = {}
                sU_ev = {}
                tp_ev = {}
                fi = 0
                for c in range(NCH):
                    for f in range(NF):
                        b = fi % 2
                        rows = f * 1024
                        cols = c * IC
                        wsrc = w_f32[rows:rows + 1024, cols:cols + IC]
                        wsrcB = w_bf16[rows:rows + 1024, cols:cols + IC]
                        wv = wsrc.rearrange("(a p) n -> p a n", p=128)
                        wvB = wsrcB.rearrange("(a p) n -> p a n", p=128)
                        dstf = wf32[:, b, :].rearrange("p (a n) -> p a n", a=NKW)
                        dstb = wbf[:, b, :].rearrange("p (a n) -> p a n", a=NKW)
                        wts = []
                        if fi >= 2:
                            wts.append(cons_ev[fi - 2])
                        if st == 0:
                            if fi >= 2:
                                wts.append(scr_ev[fi - 2])
                            ed = E("sync", lambda e, wv=wv, dstf=dstf:
                                   e.dma_start(out=dstf, in_=wv),
                                   w=wts, inc=("qW", 16))
                            ec = E("vector", lambda e, dstf=dstf, dstb=dstb:
                                   e.tensor_copy(out=dstb, in_=dstf),
                                   w=[ed], inc=("vS", 1))
                            es2 = E("gpsimd", lambda e, wvB=wvB, dstb=dstb:
                                    e.dma_start(out=wvB, in_=dstb),
                                    w=[ec], inc=("qG", 16))
                            scr_ev[fi] = es2
                            fill_ev[fi] = ec
                        else:
                            ed = E("sync", lambda e, wvB=wvB, dstb=dstb:
                                   e.dma_start(out=dstb, in_=wvB),
                                   w=wts, inc=("qW", 16))
                            fill_ev[fi] = ed
                        last_mm = None
                        for kk in range(NKW):
                            kg = f * NKW + kk
                            for tau in range(NT):
                                startf = (f == 0 and kk == 0)
                                stopf = (f == NF - 1 and kk == NKW - 1)
                                wmm = [fill_ev[fi]] if kk == 0 and tau == 0 else []
                                if startf and tau == 0 and c >= 1:
                                    if is_g:
                                        wmm += [evac_ev[(c - 1, t2)]
                                                for t2 in range(NT)]
                                    else:
                                        wmm += [sU_ev[(c - 1, t2)]
                                                for t2 in range(NT)]
                                last_mm = E("tensor",
                                            lambda e, kg=kg, tau=tau, b=b, kk=kk,
                                            startf=startf, stopf=stopf, c=c:
                                            e.matmul(
                                                out=pA[tau][:, 0:IC],
                                                lhsT=xa[:, kg,
                                                        tau * 128:tau * 128 + 128],
                                                rhs=wbf[:, b,
                                                        kk * IC:kk * IC + IC],
                                                start=startf, stop=stopf),
                                            w=wmm,
                                            inc=("tS", 1) if (stopf or
                                                              (kk == NKW - 1 and
                                                               tau == NT - 1))
                                            else None)
                        cons_ev[fi] = ("tS", cnt["tS"])
                        fi += 1
                    grp = tpad()
                    if is_g:
                        for tau in range(NT):
                            act_guard(ACT.Silu)
                            evac_ev[(c, tau)] = E(
                                "scalar", lambda e, c=c, tau=tau: e.activation(
                                    out=sS[:, tau, c * IC:c * IC + IC],
                                    in_=pA[tau][:, 0:IC], func=ACT.Silu),
                                w=[grp], inc=("sS", 1))
                    else:
                        for tau in range(NT):
                            scs = (c * NT + tau) % 2
                            a0 = sc[:, 2 * scs, :]
                            a1 = sc[:, 2 * scs + 1, :]
                            ssl = sS[:, tau, c * IC:c * IC + IC]
                            wts2 = [grp]
                            if (c, tau) in tp_ev:
                                pass
                            key_prev = (c * NT + tau - 2)
                            if key_prev >= 0:
                                cp, tp_ = divmod(key_prev, NT)
                                wts2.append(tp_ev[(cp, tp_)])
                            E("vector", lambda e, a0=a0, ssl=ssl:
                              e.scalar_tensor_tensor(
                                  out=a0, in0=ssl, scalar=-1.0, in1=ssl,
                                  op0=ALU.mult, op1=ALU.max), w=wts2)
                            E("vector", lambda e, a0=a0, a1=a1, tau=tau:
                              e.tensor_scalar(
                                  out=a1, in0=a0, scalar1=tthr[:, tau:tau + 1],
                                  scalar2=None, op0=ALU.is_gt))
                            E("vector", lambda e, a0=a0, a1=a1, ssl=ssl:
                              e.tensor_mul(out=a0, in0=ssl, in1=a1))
                            esu = E("vector", lambda e, a0=a0, a1=a1, tau=tau:
                                    e.tensor_mul(out=a1, in0=pA[tau][:, 0:IC],
                                                 in1=a0),
                                    inc=("vS", 1))
                            sU_ev[(c, tau)] = esu
                            last_tp = None
                            for k in range(2):
                                bb = k % 2
                                et2 = E("tensor", lambda e, a1=a1, k=k, bb=bb:
                                        e.matmul(
                                            out=pT[bb][:],
                                            lhsT=a1[:, 128 * k:128 * k + 128],
                                            rhs=ident[:], start=True,
                                            stop=True),
                                        w=[esu], inc=("tS", 1))
                                last_tp = E("vector",
                                            lambda e, c=c, tau=tau, k=k, bb=bb:
                                            e.tensor_copy(
                                                out=sS[:, tau,
                                                       c * IC + 128 * k:
                                                       c * IC + 128 * k + 128],
                                                in_=pT[bb][:]),
                                            w=[et2], inc=("vS", 1))
                            tp_ev[(c, tau)] = last_tp

            mm_phase(wgT, wgB, is_g=True)

            p2s = snap()
            enter_phase(p2s, engines=("scalar", "vector"))
            for pair in range(2):
                ta, tb = 2 * pair, 2 * pair + 1
                ea = E("vector", lambda e, ta=ta: e.scalar_tensor_tensor(
                    out=b2[:, 0, :], in0=sS[:, ta, :], scalar=-1.0,
                    in1=sS[:, ta, :], op0=ALU.mult, op1=ALU.max),
                    w=[("sS", cnt["sS"])] if pair == 1 else [],
                    inc=("vS", 1))
                eb = E("vector", lambda e, tb=tb: e.scalar_tensor_tensor(
                    out=b2[:, 1, :], in0=sS[:, tb, :], scalar=-1.0,
                    in1=sS[:, tb, :], op0=ALU.mult, op1=ALU.max),
                    inc=("vS", 1))
                E("vector", lambda e: e.memset(lohi[:, 0:2], 0.0))
                E("vector", lambda e: e.reduce_max(out=lohi[:, 2:3],
                                                   in_=b2[:, 0, :], axis=AX))
                ehi = E("vector", lambda e: e.reduce_max(out=lohi[:, 3:4],
                                                         in_=b2[:, 1, :],
                                                         axis=AX),
                        inc=("vS", 1))
                for j in range(JB):
                    E("vector", lambda e: e.tensor_add(
                        out=lohi[:, 4:6], in0=lohi[:, 0:2], in1=lohi[:, 2:4]),
                        w=[ehi] if j == 0 else [])
                    E("vector", lambda e: e.tensor_scalar_mul(
                        out=lohi[:, 4:6], in0=lohi[:, 4:6], scalar1=0.5))
                    E("vector", lambda e: e.tensor_scalar_mul(
                        out=lohi[:, 6:8], in0=lohi[:, 4:6], scalar1=-1.0),
                        inc=("vS", 1))
                    emid = vpad()
                    act_guard(ACT.Sign)
                    ecs = E("scalar", lambda e: e.activation(
                        out=junk[:], in_=b2[:, 0, :], func=ACT.Sign,
                        bias=lohi[:, 6:7], accum_out=cnt2[:, 0:1]),
                        w=[emid, ea], inc=("sS", 1))
                    ecv = E("vector", lambda e: e.tensor_scalar(
                        out=junk[:], in0=b2[:, 1, :], scalar1=lohi[:, 5:6],
                        scalar2=None, op0=ALU.is_gt, op1=ALU.add,
                        accum_out=cnt2[:, 1:2]), w=[eb] if j == 0 else [])
                    E("vector", lambda e: e.tensor_scalar(
                        out=cmsk[:, 0:1], in0=cnt2[:, 0:1], scalar1=1.5,
                        scalar2=None, op0=ALU.is_gt), w=[ecs])
                    E("vector", lambda e: e.tensor_scalar(
                        out=cmsk[:, 1:2], in0=cnt2[:, 1:2],
                        scalar1=float(KTGT) + 0.5, scalar2=None,
                        op0=ALU.is_gt))
                    E("vector", lambda e: e.tensor_scalar(
                        out=cmsk[:, 2:3], in0=cnt2[:, 0:1], scalar1=1.5,
                        scalar2=None, op0=ALU.is_le))
                    E("vector", lambda e: e.tensor_scalar(
                        out=cmsk[:, 3:4], in0=cnt2[:, 1:2],
                        scalar1=float(KTGT) + 0.5, scalar2=None,
                        op0=ALU.is_le))
                    E("vector", lambda e: e.select(
                        out=lohi[:, 0:2], mask=cmsk[:, 0:2],
                        on_true=lohi[:, 4:6], on_false=lohi[:, 0:2]))
                    ehi = E("vector", lambda e: e.select(
                        out=lohi[:, 2:4], mask=cmsk[:, 2:4],
                        on_true=lohi[:, 4:6], on_false=lohi[:, 2:4]),
                        inc=("vS", 1))
                    if dbg == 1 and pair == 1 and j < 33:
                        E("vector", lambda e, j=j: e.tensor_copy(
                            out=trc[:, j * 12:j * 12 + 2], in_=cnt2[:, 0:2]))
                        E("vector", lambda e, j=j: e.tensor_copy(
                            out=trc[:, j * 12 + 2:j * 12 + 4],
                            in_=cmsk[:, 0:2]))
                        E("vector", lambda e, j=j: e.tensor_copy(
                            out=trc[:, j * 12 + 4:j * 12 + 12], in_=lohi[:]))
                E("vector", lambda e, ta=ta: e.tensor_copy(
                    out=tthr[:, ta:ta + 2], in_=lohi[:, 2:4]), w=[ehi],
                    inc=("vS", 1))

            if dbg == 1:
                dsn = snap()
                enter_phase(dsn, engines=("gpsimd",))
                E("gpsimd", lambda e: e.dma_start(out=dbg_tr, in_=trc[:]),
                  inc=("qG", 16))
                E("gpsimd", lambda e: e.dma_start(out=dbg_s, in_=sS[:]),
                  inc=("qG", 16))
                E("gpsimd", lambda e: e.dma_start(out=dbg_t, in_=tthr[:]),
                  inc=("qG", 16))
                E("gpsimd", lambda e: e.dma_start(out=dbg_a, in_=attn[:]),
                  inc=("qG", 16))
                break
            mm_phase(wuT, wuB, is_g=False)

            p3b = snap()
            enter_phase(p3b, engines=("gpsimd", "vector"))
            for half in range(4):
                rows = half * 256
                wv = wb_h[rows:rows + 256, :].rearrange("(a p) n -> p a n", p=128)
                bslot = half % 2
                wts3 = []
                if half >= 2:
                    wts3.append(wbc_ev[bslot])
                ewb = E("gpsimd", lambda e, wv=wv, bslot=bslot: e.dma_start(
                    out=wf32[:, bslot, :].rearrange("p (a n) -> p a n", a=2),
                    in_=wv), w=wts3, inc=("qG", 16))
                ec2 = E("vector", lambda e, half=half, bslot=bslot:
                        e.tensor_copy(
                            out=wbV[:, 2 * half:2 * half + 2, :],
                            in_=wf32[:, bslot, :].rearrange(
                                "p (a n) -> p a n", a=2)),
                        w=[ewb], inc=("vS", 1))
                if half == 0:
                    wbc_ev = [None, None]
                wbc_ev[bslot] = ec2

            p4s = snap()
            enter_phase(p4s)
            fill_ev = {}
            cons_ev = {}
            scr_ev = {}
            gi = 0
            evac4 = {}
            for h in range(NHC):
                for g in range(NPAIR):
                    b = gi % 2
                    rows = g * 256
                    cols = h * HC
                    wsrc = wdT[rows:rows + 256, cols:cols + HC]
                    wsrcB = wdB[rows:rows + 256, cols:cols + HC]
                    wv = wsrc.rearrange("(a p) n -> p a n", p=128)
                    wvB = wsrcB.rearrange("(a p) n -> p a n", p=128)
                    dstf = wf32[:, b, 0:1024].rearrange("p (a n) -> p a n", a=2)
                    dstb = wbf[:, b, 0:1024].rearrange("p (a n) -> p a n", a=2)
                    wts = []
                    if gi >= 2:
                        wts.append(cons_ev[gi - 2])
                    if st == 0:
                        if gi >= 2:
                            wts.append(scr_ev[gi - 2])
                        ed = E("sync", lambda e, wv=wv, dstf=dstf:
                               e.dma_start(out=dstf, in_=wv), w=wts,
                               inc=("qW", 16))
                        ec = E("vector", lambda e, dstf=dstf, dstb=dstb:
                               e.tensor_copy(out=dstb, in_=dstf), w=[ed],
                               inc=("vS", 1))
                        es3 = E("gpsimd", lambda e, wvB=wvB, dstb=dstb:
                                e.dma_start(out=wvB, in_=dstb), w=[ec],
                                inc=("qG", 16))
                        scr_ev[gi] = es3
                        fill_ev[gi] = ec
                    else:
                        ed = E("sync", lambda e, wvB=wvB, dstb=dstb:
                               e.dma_start(out=dstb, in_=wvB), w=wts,
                               inc=("qW", 16))
                        fill_ev[gi] = ed
                    for sub in range(2):
                        s_ = 2 * g + sub
                        for tau in range(NT):
                            startf = (g == 0 and sub == 0)
                            stopf = (g == NPAIR - 1 and sub == 1)
                            wmm = [fill_ev[gi]] if sub == 0 and tau == 0 else []
                            if startf and tau == 0 and h >= 1:
                                wmm += [evac4[(h - 1, t2)] for t2 in range(NT)]
                            E("tensor", lambda e, s_=s_, tau=tau, b=b, sub=sub,
                              startf=startf, stopf=stopf: e.matmul(
                                  out=pA[tau][:],
                                  lhsT=sS[:, tau, 128 * s_:128 * s_ + 128],
                                  rhs=wbf[:, b, 512 * sub:512 * sub + 512],
                                  start=startf, stop=stopf),
                              w=wmm,
                              inc=("tS", 1) if (stopf or (sub == 1 and
                                                          tau == NT - 1))
                              else None)
                    cons_ev[gi] = ("tS", cnt["tS"])
                    gi += 1
                grp = tpad()
                for tau in range(NT):
                    evac4[(h, tau)] = E(
                        "vector", lambda e, h=h, tau=tau: e.tensor_copy(
                            out=attn[:, tau, h * HC:h * HC + HC],
                            in_=pA[tau][:]), w=[grp], inc=("vS", 1))

            if dbg == 2:
                dsn = snap()
                enter_phase(dsn, engines=("gpsimd",))
                E("gpsimd", lambda e: e.dma_start(out=dbg_s, in_=sS[:]),
                  inc=("qG", 16))
                E("gpsimd", lambda e: e.dma_start(out=dbg_t, in_=tthr[:]),
                  inc=("qG", 16))
                E("gpsimd", lambda e: e.dma_start(out=dbg_a, in_=attn[:]),
                  inc=("qG", 16))
                break
            p5s = snap()
            enter_phase(p5s)
            prev_out_dma = None
            logit_ev = []

            prev_gate2 = None
            ch = 0
            for tau in range(NT):
                r0 = base_tok + tau * 128
                wx = [prev_gate2] if prev_gate2 else []
                ex = E("gpsimd", lambda e, r0=r0: e.dma_start(
                    out=xtv, in_=x_in[r0:r0 + 128, :]), w=wx, inc=("qG", 16))
                b0 = (ch % 2) * 8
                ch += 1
                esd = ln_prep(attn[:, tau, :], [], b0)
                eln = ln_norm(lnb, attn[:, tau, :], esd, b0)
                E("vector", lambda e, tau=tau: e.tensor_add(
                    out=x1v[:, tau, :], in0=xtv, in1=lnb), w=[ex],
                    inc=("vS", 1))
                ex1 = vpad()
                prev_gate2 = ex1
                b1 = (ch % 2) * 8
                ch += 1
                esd2 = ln_prep(x1v[:, tau, :], [ex1], b1)
                exl2 = ln_norm(xl2v[:, tau, :], x1v[:, tau, :], esd2, b1)
                b2_ = (ch % 2) * 8
                ch += 1
                esd3 = ln_prep(xl2v[:, tau, :], [exl2], b2_, rms_only=True)
                exn = ln_norm(xnb, xl2v[:, tau, :], esd3, b2_, rms_only=True)
                evk2 = [None, None]
                for k in range(32):
                    b = k % 2
                    wtp = [exn]
                    if k >= 2:
                        wtp.append(evk2[b])
                    et = E("tensor", lambda e, k=k, b=b: e.matmul(
                        out=pT[b][:], lhsT=xnb[:, 128 * k:128 * k + 128],
                        rhs=ident[:], start=True, stop=True),
                        w=wtp, inc=("tS", 1))
                    evk2[b] = E("vector", lambda e, k=k, b=b, tau=tau:
                                e.tensor_copy(
                                    out=xnTs[tau][:, k, :], in_=pT[b][:]),
                                w=[et], inc=("vS", 1))
                elast = vpad()
                for k in range(32):
                    E("tensor", lambda e, k=k, tau=tau: e.matmul(
                        out=pA[2][:, 0:32], lhsT=xnTs[tau][:, k, :],
                        rhs=wallB[:, k, :], start=(k == 0), stop=(k == 31)),
                        w=[elast] if k == 0 else [],
                        inc=("tS", 1) if k == 31 else None)
                emm = tpad()
                ecp = E("vector", lambda e, tau=tau: e.tensor_copy(
                    out=logi[:, tau, :], in_=pA[2][:, 0:24]), w=[emm])
                if dbg == 5 and tau == 0:
                    E("vector", lambda e: e.tensor_copy(
                        out=logi[:, 3, :], in_=pA[2][:, 0:24]))
                    dsnP = snap()
                    enter_phase(dsnP, engines=("gpsimd",))
                    E("gpsimd", lambda e: e.dma_start(
                        out=dbg_g[:, 5, :], in_=logi[:, 3, :]),
                        inc=("qG", 16))
                    E("gpsimd", lambda e: e.dma_start(
                        out=dbg_g[:, 6, :], in_=logi[:, 0, :]),
                        inc=("qG", 16))
                E("vector", lambda e, tau=tau: e.tensor_add(
                    out=logi[:, tau, 0:16], in0=logi[:, tau, 0:16],
                    in1=bres[:]))
                E("vector", lambda e, tau=tau: e.tensor_add(
                    out=logi[:, tau, 16:20], in0=logi[:, tau, 16:20],
                    in1=bpre[:]))
                E("vector", lambda e, tau=tau: e.tensor_add(
                    out=logi[:, tau, 20:24], in0=logi[:, tau, 20:24],
                    in1=bpost[:]))
                logit_ev.append(E("vector", lambda e, tau=tau:
                                  e.tensor_scalar_mul(
                                      out=zsk[:, tau, :],
                                      in0=logi[:, tau, 0:16],
                                      scalar1=TAU_INV), inc=("vS", 1)))
                if dbg == 5:
                    dsnA = snap()
                    enter_phase(dsnA, engines=("gpsimd",))
                    E("gpsimd", lambda e, tau=tau: e.dma_start(
                        out=dbg_g[:, tau, :], in_=logi[:, 0, :]),
                        inc=("qG", 16))
                prev_out_dma = None

            zv = zsk[:].rearrange("p a (i j) -> p a i j", i=4)
            mv = msk[:].rearrange("p a (i j) -> p a i j", i=4)
            last_e = logit_ev[-1]
            for it in range(NSK):
                if it == 0:
                    E("vector", lambda e: e.tensor_copy(
                        out=msk[:], in_=zsk[:]), w=logit_ev, inc=("vS", 1))
                else:
                    for i_ in range(4):
                        E("vector", lambda e, i_=i_: e.tensor_sub(
                            out=mv[:, :, i_, :], in0=zv[:, :, i_, :],
                            in1=lmv[:].rearrange("p a j -> p a j")),
                            w=[last_e] if i_ == 0 else [],
                            inc=("vS", 1) if i_ == 3 else None)
                E("vector", lambda e: e.tensor_reduce(
                    out=m1s[:], in_=mv, axis=AX, op=ALU.max))
                for j_ in range(4):
                    E("vector", lambda e, j_=j_: e.tensor_sub(
                        out=mv[:, :, :, j_], in0=mv[:, :, :, j_], in1=m1s[:]))
                eM = vpad()
                act_guard(ACT.Exp)
                eE = E("scalar", lambda e: e.activation(
                    out=msk[:], in_=msk[:], func=ACT.Exp), w=[eM],
                    inc=("sS", 1))
                if dbg == 5 and it == 0:
                    dsn5 = snap()
                    enter_phase(dsn5, engines=("gpsimd",))
                    E("gpsimd", lambda e: e.dma_start(
                        out=dbg_g[:, 4, :], in_=logi[:, 0, :]),
                        inc=("qG", 16))
                    E("gpsimd", lambda e: e.dma_start(
                        out=sSf[:, 0:768].rearrange("p x -> p x") if False
                        else dbg_s[:].rearrange("p a b -> p (a b)")[:, 0:1024],
                        in_=wallB[:].rearrange("p k n -> p (k n)")),
                        inc=("qG", 16))
                    E("gpsimd", lambda e: e.dma_start(
                        out=dbg_s[:].rearrange("p a b -> p (a b)")[:, 1024:5120],
                        in_=xnTs[0][:].rearrange("p k t -> p (k t)")),
                        inc=("qG", 16))
                    E("gpsimd", lambda e: e.dma_start(
                        out=dbg_f[:, :, 64:80], in_=msk[:]), inc=("qG", 16))
                    E("gpsimd", lambda e: e.dma_start(
                        out=dbg_f[:, :, 24:40], in_=zsk[:]), inc=("qG", 16))
                    E("gpsimd", lambda e: e.dma_start(
                        out=dbg_f[:, :, 0:24], in_=logi[:]), inc=("qG", 16))
                    break
                E("vector", lambda e: e.tensor_reduce(
                    out=ssk[:].rearrange("p a j -> p (a j)").rearrange(
                        "p (a i) -> p a i", a=4),
                    in_=mv, axis=AX, op=ALU.add), w=[eE], inc=("vS", 1))
                eS = vpad()
                act_guard(ACT.Ln)
                eL0 = E("scalar", lambda e: e.activation(
                    out=lmr[:], in_=ssk[:], func=ACT.Ln), w=[eS],
                    inc=("sS", 1))
                eL = E("vector", lambda e: e.tensor_add(
                    out=lmu[:], in0=lmr[:], in1=m1s[:]), w=[eL0],
                    inc=("vS", 1))
                for j_ in range(4):
                    E("vector", lambda e, j_=j_: e.tensor_sub(
                        out=mv[:, :, :, j_], in0=zv[:, :, :, j_], in1=lmu[:]),
                        w=[eL] if j_ == 0 else [],
                        inc=("vS", 1) if j_ == 3 else None)
                mvt = msk[:].rearrange("p a (i j) -> p a j i", i=4)
                E("vector", lambda e: e.tensor_reduce(
                    out=m1s[:], in_=mvt, axis=AX, op=ALU.max))
                for i_ in range(4):
                    E("vector", lambda e, i_=i_: e.tensor_sub(
                        out=mv[:, :, i_, :], in0=mv[:, :, i_, :], in1=m1s[:]))
                eM2 = vpad()
                act_guard(ACT.Exp)
                eE2 = E("scalar", lambda e: e.activation(
                    out=msk[:], in_=msk[:], func=ACT.Exp), w=[eM2],
                    inc=("sS", 1))
                E("vector", lambda e: e.tensor_reduce(
                    out=ssk[:], in_=mvt, axis=AX, op=ALU.add), w=[eE2],
                    inc=("vS", 1))
                eS2 = vpad()
                act_guard(ACT.Ln)
                eL2 = E("scalar", lambda e: e.activation(
                    out=lmr[:], in_=ssk[:], func=ACT.Ln), w=[eS2],
                    inc=("sS", 1))
                last_e = E("vector", lambda e: e.tensor_add(
                    out=lmv[:], in0=lmr[:], in1=m1s[:]), w=[eL2],
                    inc=("vS", 1))
                if dbg == 4 and it < 50:
                    E("vector", lambda e, it=it: e.tensor_copy(
                        out=trc[:, it * 8:it * 8 + 4], in_=lmu[:, 0, :]),
                        w=[last_e])
                    E("vector", lambda e, it=it: e.tensor_copy(
                        out=trc[:, it * 8 + 4:it * 8 + 8], in_=lmv[:, 0, :]))
            if dbg == 5:
                break
            for i_ in range(4):
                E("vector", lambda e, i_=i_: e.tensor_sub(
                    out=mv[:, :, i_, :], in0=zv[:, :, i_, :], in1=lmv[:]),
                    w=[last_e] if i_ == 0 else [])
            for j_ in range(4):
                E("vector", lambda e, j_=j_: e.tensor_sub(
                    out=mv[:, :, :, j_], in0=mv[:, :, :, j_], in1=lmu[:]),
                    inc=("vS", 1) if j_ == 3 else None)
            ehj = vpad()
            act_guard(ACT.Exp)
            eH = E("scalar", lambda e: e.activation(
                out=msk[:], in_=msk[:], func=ACT.Exp), w=[ehj],
                inc=("sS", 1))
            eHp = E("vector", lambda e: e.tensor_scalar(
                out=hrs[:], in0=msk[:], scalar1=gsc[:, 1:2], scalar2=None,
                op0=ALU.mult), w=[eH])
            for tau in range(NT):
                eHp = E("vector", lambda e, tau=tau: e.tensor_add(
                    out=hrs[:, tau, :], in0=hrs[:, tau, :], in1=gig[:]),
                    inc=("vS", 1) if tau == NT - 1 else None)
            E("vector", lambda e: e.tensor_reduce(
                out=m1s[:].rearrange("p a j -> p (a j)")[:, 0:4].rearrange(
                    "p (a o) -> p a o", o=1),
                in_=logi[:, :, 16:20], axis=AX, op=ALU.max))
            E("vector", lambda e: e.tensor_reduce(
                out=m1s[:].rearrange("p a j -> p (a j)")[:, 4:8].rearrange(
                    "p (a o) -> p a o", o=1),
                in_=logi[:, :, 20:24], axis=AX, op=ALU.max))
            m1f = m1s[:].rearrange("p a j -> p (a j)")
            for tau in range(NT):
                E("vector", lambda e, tau=tau: e.tensor_scalar(
                    out=hpre[:, tau, :], in0=logi[:, tau, 16:20],
                    scalar1=m1f[:, tau:tau + 1], scalar2=None,
                    op0=ALU.subtract))
                E("vector", lambda e, tau=tau: e.tensor_scalar(
                    out=hpost[:, tau, :], in0=logi[:, tau, 20:24],
                    scalar1=m1f[:, 4 + tau:5 + tau], scalar2=None,
                    op0=ALU.subtract))
            epadx = vpad()
            act_guard(ACT.Exp)
            epre = E("scalar", lambda e: e.activation(
                out=hpre[:], in_=hpre[:], func=ACT.Exp), w=[epadx],
                inc=("sS", 1))
            epost = E("scalar", lambda e: e.activation(
                out=hpost[:], in_=hpost[:], func=ACT.Exp), inc=("sS", 1))
            es1 = E("vector", lambda e: e.tensor_reduce(
                out=ssk[:].rearrange("p a j -> p (a j)")[:, 0:4].rearrange(
                    "p (a o) -> p a o", o=1),
                in_=hpre[:], axis=AX, op=ALU.add), w=[epre])
            es2b = E("vector", lambda e: e.tensor_reduce(
                out=ssk[:].rearrange("p a j -> p (a j)")[:, 4:8].rearrange(
                    "p (a o) -> p a o", o=1),
                in_=hpost[:], axis=AX, op=ALU.add), w=[epost])
            sskf = ssk[:].rearrange("p a j -> p (a j)")
            erc = E("vector", lambda e: e.reciprocal(
                out=sskf[:, 8:16], in_=sskf[:, 0:8]), w=[es1, es2b])
            hnorm_ev = None
            for tau in range(NT):
                E("vector", lambda e, tau=tau: e.tensor_scalar(
                    out=hpre[:, tau, :], in0=hpre[:, tau, :],
                    scalar1=sskf[:, 8 + tau:9 + tau], scalar2=None,
                    op0=ALU.mult), w=[erc] if tau == 0 else [])
                hnorm_ev = E("vector", lambda e, tau=tau: e.tensor_scalar(
                    out=hpost[:, tau, :], in0=hpost[:, tau, :],
                    scalar1=sskf[:, 12 + tau:13 + tau], scalar2=None,
                    op0=ALU.mult),
                    inc=("vS", 1) if tau == NT - 1 else None)

            prev_out = None
            for tau in range(NT):
                e0 = E("vector", lambda e, tau=tau: e.tensor_scalar(
                    out=xpre[:], in0=xl2v[:, tau, 0:1024],
                    scalar1=hpre[:, tau, 0:1], scalar2=None, op0=ALU.mult),
                    w=[hnorm_ev])
                for n in range(1, 4):
                    e0 = E("vector", lambda e, tau=tau, n=n:
                           e.scalar_tensor_tensor(
                               out=xpre[:], in0=xl2v[:, tau,
                                                     1024 * n:1024 * n + 1024],
                               scalar=hpre[:, tau, n:n + 1], in1=xpre[:],
                               op0=ALU.mult, op1=ALU.add))
                ecb = E("vector", lambda e: e.tensor_copy(out=xpb, in_=xpre[:]),
                        inc=("vS", 1))
                evk3 = [None, None]
                for k in range(8):
                    b = k % 2
                    wtp = [ecb]
                    if k >= 2:
                        wtp.append(evk3[b])
                    et = E("tensor", lambda e, k=k, b=b: e.matmul(
                        out=pT[b][:], lhsT=xpb[:, 128 * k:128 * k + 128],
                        rhs=ident[:], start=True, stop=True),
                        w=wtp, inc=("tS", 1))
                    evk3[b] = E("vector", lambda e, k=k, b=b: e.tensor_copy(
                        out=xpT[:, k, :], in_=pT[b][:]), w=[et],
                        inc=("vS", 1))
                eyl = []
                etp = vpad()
                for half in range(2):
                    for k in range(8):
                        E("tensor", lambda e, k=k, half=half: e.matmul(
                            out=pA[half][:],
                            lhsT=xpT[:, k, :],
                            rhs=wbV[:, k, 512 * half:512 * half + 512],
                            start=(k == 0), stop=(k == 7)),
                            w=[etp] if k == 0 else [],
                            inc=("tS", 1) if k == 7 else None)
                    emmy = tpad()
                    eyl.append(E("vector", lambda e, half=half: e.tensor_copy(
                        out=ybv[:, 512 * half:512 * half + 512],
                        in_=pA[half][:]), w=[emmy], inc=("vS", 1)))
                hrv = hrs[:].rearrange("p a (i j) -> p a i j", i=4)
                efo = None
                for i_ in range(4):
                    ei = E("vector", lambda e, tau=tau, i_=i_: e.tensor_scalar(
                        out=xpre[:], in0=xl2v[:, tau, 0:1024],
                        scalar1=hrv[:, tau, i_, 0:1], scalar2=None,
                        op0=ALU.mult), w=eyl if i_ == 0 else [])
                    for j_ in range(1, 4):
                        ei = E("vector", lambda e, tau=tau, i_=i_, j_=j_:
                               e.scalar_tensor_tensor(
                                   out=xpre[:],
                                   in0=xl2v[:, tau, 1024 * j_:1024 * j_ + 1024],
                                   scalar=hrv[:, tau, i_, j_:j_ + 1],
                                   in1=xpre[:], op0=ALU.mult, op1=ALU.add))
                    efo = E("vector", lambda e, tau=tau, i_=i_:
                            e.scalar_tensor_tensor(
                                out=fov[:, 1024 * i_:1024 * i_ + 1024],
                                in0=ybv, scalar=hpost[:, tau, i_:i_ + 1],
                                in1=xpre[:], op0=ALU.mult, op1=ALU.add),
                            inc=("vS", 1))
                bfn = ((tau % 2)) * 8
                estf = ln_prep(fov, [efo], bfn)
                wxt = [prev_out] if prev_out is not None else []
                ln_norm(xtv, fov, estf, bfn, waits=wxt)
                E("vector", lambda e, tau=tau: e.tensor_add(
                    out=xtv, in0=xtv, in1=x1v[:, tau, :]), inc=("vS", 1))
                ex2 = vpad()
                r0 = base_tok + tau * 128
                prev_out = E("gpsimd", lambda e, r0=r0: e.dma_start(
                    out=y_out[r0:r0 + 128, :], in_=xtv), w=[ex2],
                    inc=("qG", 16))

            if dbg == 4:
                dsn = snap()
                enter_phase(dsn, engines=("gpsimd",))
                for nm, src in (("dbg_s", sS[:]), ("dbg_t", tthr[:]),
                                ("dbg_a", attn[:])):
                    pass
                E("gpsimd", lambda e: e.dma_start(out=dbg_t, in_=tthr[:]),
                  inc=("qG", 16))
                E("gpsimd", lambda e: e.dma_start(out=dbg_a, in_=attn[:]),
                  inc=("qG", 16))
                E("gpsimd", lambda e: e.dma_start(
                    out=dbg_f[:, :, 0:24], in_=logi[:]), inc=("qG", 16))
                E("gpsimd", lambda e: e.dma_start(
                    out=dbg_f[:, :, 24:40], in_=zsk[:]), inc=("qG", 16))
                E("gpsimd", lambda e: e.dma_start(
                    out=dbg_f[:, :, 40:56], in_=hrs[:]), inc=("qG", 16))
                E("gpsimd", lambda e: e.dma_start(
                    out=dbg_f[:, :, 56:60], in_=hpre[:]), inc=("qG", 16))
                E("gpsimd", lambda e: e.dma_start(
                    out=dbg_f[:, :, 60:64], in_=hpost[:]), inc=("qG", 16))
                E("gpsimd", lambda e: e.dma_start(
                    out=dbg_f[:, :, 64:80], in_=msk[:]), inc=("qG", 16))
                for _a in range(NT):
                    E("gpsimd", lambda e, _a=_a: e.dma_start(
                        out=dbg_f[:, _a, 80:96], in_=gig[:]), inc=("qG", 16))
                E("gpsimd", lambda e: e.dma_start(out=dbg_s, in_=sS[:]),
                  inc=("qG", 16))
                E("gpsimd", lambda e: e.dma_start(out=dbg_tr, in_=trc[:]),
                  inc=("qG", 16))
                E("gpsimd", lambda e: e.dma_start(
                    out=dbg_g[:, 0:4, :], in_=logi[:]), inc=("qG", 16))
                E("gpsimd", lambda e: e.dma_start(
                    out=dbg_g[:, 4:8, :], in_=logi[:]), inc=("qG", 16))
                E("gpsimd", lambda e: e.dma_start(out=dbg_x, in_=xtv),
                  inc=("qG", 16))
                E("gpsimd", lambda e: e.dma_start(out=dbg_st, in_=st1[:]),
                  inc=("qG", 16))
                E("gpsimd", lambda e: e.dma_start(out=dbg_ln, in_=lnb),
                  inc=("qG", 16))
                break

        E("gpsimd", lambda e: None, w=[("qG", cnt["qG"])])

        for s, v in all_waits:
            assert v <= cnt[s], f"wait {s}>={v} unreachable (final {cnt[s]})"

        def run_stream(eng_name):
            def body(e):
                for ww, fn, inc in sched[eng_name]:
                    for s, v in ww:
                        e.wait_ge(SEM[s], v)
                    ins = fn(e)
                    if inc is not None:
                        assert ins is not None
                        ins.then_inc(SEM[inc[0]], inc[1])
            return body

        blk.sync(run_stream("sync"))
        blk.gpsimd(run_stream("gpsimd"))
        blk.scalar(run_stream("scalar"))
        blk.vector(run_stream("vector"))
        blk.tensor(run_stream("tensor"))
    return nc


_NC_CACHE = None


def kernel(L0_attn_x, L0_attn_w_gate, L0_attn_w_up, L0_attn_w_down,
           L0_ffn_W_res, L0_ffn_W_pre, L0_ffn_W_post, L0_ffn_H_res_base,
           L0_ffn_H_pre_base, L0_ffn_H_post_base, L0_ffn_gate,
           L0_ffn_W_branch):
    global _NC_CACHE
    x = np.ascontiguousarray(np.asarray(L0_attn_x, dtype=np.float32)
                             .reshape(8192, 4096))
    wgT = np.ascontiguousarray(np.asarray(L0_attn_w_gate, np.float32).T)
    wuT = np.ascontiguousarray(np.asarray(L0_attn_w_up, np.float32).T)
    wdT = np.ascontiguousarray(np.asarray(L0_attn_w_down, np.float32).T)
    wb = np.ascontiguousarray(np.asarray(L0_ffn_W_branch, np.float32))
    wall = np.concatenate([np.asarray(L0_ffn_W_res, np.float32),
                           np.asarray(L0_ffn_W_pre, np.float32),
                           np.asarray(L0_ffn_W_post, np.float32),
                           np.zeros((8, 4096), np.float32)], axis=0)
    wallT = np.ascontiguousarray(wall.T)
    bres = np.ascontiguousarray(np.tile(
        np.asarray(L0_ffn_H_res_base, np.float32).reshape(1, 16), (128, 1)))
    bpre = np.ascontiguousarray(np.tile(
        np.asarray(L0_ffn_H_pre_base, np.float32).reshape(1, 4), (128, 1)))
    bpost = np.ascontiguousarray(np.tile(
        np.asarray(L0_ffn_H_post_base, np.float32).reshape(1, 4), (128, 1)))
    eye4 = np.ascontiguousarray(np.tile(
        np.eye(4, dtype=np.float32).reshape(1, 16), (128, 1)))
    gate = np.ascontiguousarray(np.tile(
        np.asarray(L0_ffn_gate, np.float32).reshape(1, 1), (128, 1)))
    ident = np.eye(128, dtype=ml_dtypes.bfloat16)

    if _NC_CACHE is None:
        _NC_CACHE = build_nc()
    nc = _NC_CACHE

    in_maps = []
    for core in range(8):
        in_maps.append({
            "x_in": x[core * 1024:(core + 1) * 1024, :],
            "wgT": wgT, "wuT": wuT, "wdT": wdT, "wb": wb, "wallT": wallT,
            "bres": bres, "bpre": bpre, "bpost": bpost, "eye4": eye4,
            "gate": gate, "ident": ident,
        })
    res = run_bass_kernel_spmd(nc, in_maps, core_ids=list(range(8)))
    out = np.concatenate([np.asarray(r["y_out"]) for r in res.results], axis=0)
    return out.reshape(2, 4096, 4096).astype(np.float32)
